# version 07
# baseline (speedup 1.0000x reference)
import numpy as np
import ml_dtypes
from contextlib import ExitStack
import concourse.bass as bass
import concourse.mybir as mybir
from concourse.bass_utils import run_bass_kernel_spmd

F32 = mybir.dt.float32
BF16 = mybir.dt.bfloat16
AF = mybir.ActivationFunctionType
ALU = mybir.AluOpType

D = 1024
T = 2048
CH = 1024
SUB = 512
NDT = 8
NHT = 32
NOT = 8
NET = 8
EPS = 1e-5


class Sem:
    def __init__(self, h):
        self.h = h
        self.n = 0

    def bump(self, k=1):
        self.n += k
        return self.n


class Prog:
    ENGINES = ("sync", "gpsimd", "vector", "scalar", "tensor")

    def __init__(self):
        self.q = {e: [] for e in self.ENGINES}

    def add(self, eng, fn):
        self.q[eng].append(fn)

    def wait(self, eng, sem, val):
        if val > 0:
            self.q[eng].append(lambda e, s=sem, v=val: e.wait_ge(s.h, v))


def build_program():
    nc = bass.Bass("TRN2", target_bir_lowering=False)

    x_cm = nc.dram_tensor("x_cm", [D, T], F32, kind="ExternalInput").ap()
    w1r = nc.dram_tensor("w1r", [NHT, 128, 1024], F32, kind="ExternalInput").ap()
    w2r = nc.dram_tensor("w2r", [NOT, 128, 4096], F32, kind="ExternalInput").ap()
    wor = nc.dram_tensor("wor", [NET, 128, 1024], F32, kind="ExternalInput").ap()
    a_pm = nc.dram_tensor("a_pm", [128, 8], F32, kind="ExternalInput").ap()
    b_pm = nc.dram_tensor("b_pm", [128, 8], F32, kind="ExternalInput").ap()
    bo_pm = nc.dram_tensor("bo_pm", [128, 8], F32, kind="ExternalInput").ap()
    b1_pm = nc.dram_tensor("b1_pm", [128, 32], F32, kind="ExternalInput").ap()
    b2_pm = nc.dram_tensor("b2_pm", [128, 8], F32, kind="ExternalInput").ap()
    scale_col = nc.dram_tensor("scale_col", [128, 1], F32, kind="ExternalInput").ap()
    mask_even = nc.dram_tensor("mask_even", [128, 1], F32, kind="ExternalInput").ap()
    mask_odd = nc.dram_tensor("mask_odd", [128, 1], F32, kind="ExternalInput").ap()
    ident_d = nc.dram_tensor("ident_d", [128, 128], BF16, kind="ExternalInput").ap()
    out_cm = nc.dram_tensor("out_cm", [D, T], F32, kind="ExternalOutput").ap()

    cc_lt_in = [nc.dram_tensor(f"cc_lt{i}_in", [16, 1024], BF16) for i in range(2)]
    cc_lt_out = [nc.dram_tensor(f"cc_lt{i}_out", [16, 1024], BF16) for i in range(2)]
    cc_u_in = nc.dram_tensor("cc_u_in", [128, 8], F32)
    cc_u_out = nc.dram_tensor("cc_u_out", [128, 8], F32)
    w1scr = nc.dram_tensor("w1scr", [NHT, 128, 1024], BF16).ap()
    w2scr = nc.dram_tensor("w2scr", [NOT, 128, 4096], BF16).ap()

    ctx = ExitStack()
    sb = lambda n, s, d=F32: ctx.enter_context(nc.sbuf_tensor(n, s, d))
    psf = lambda n, s: ctx.enter_context(nc.psum_tensor(n, s, F32))

    xres = sb("xres", [128, NDT * T])
    big = sb("big", [128, 16384])
    bigb = big[:].bitcast(BF16)
    mid = sb("mid", [128, 8192], BF16)
    xbst = sb("xbst", [128, 1024], BF16)
    x2s = sb("x2s", [128, 1024], BF16)
    ntmp = sb("ntmp", [128, 1024])
    tmp = sb("tmp", [128, 1024])
    w1b = sb("w1b", [128, 2048], BF16)
    w1e = sb("w1e", [128, 2048], BF16)
    w2b = sb("w2b", [128, 4096], BF16)
    w2e = sb("w2e", [128, 8192], BF16)
    ltA = [sb(f"ltA{i}", [65, 1024], BF16) for i in range(3)]
    ltB = [sb(f"ltB{i}", [65, 1024], BF16) for i in range(3)]
    contrib = sb("contrib", [32, 1024], BF16)
    corrt = sb("corrt", [128, 512])
    ident = sb("ident", [128, 128], BF16)
    ones_row = sb("ones_row", [1, 128])
    ones_col = sb("ones_col", [128, 1], BF16)
    ones_colf = sb("ones_colf", [128, 1])
    ahbcT = sb("ahbcT", [128, 512])
    av = sb("av", [128, 8])
    ahcol = sb("ahcol", [128, 8])
    ahp512 = sb("ahp512", [128, 8])
    bv = sb("bv", [128, 8])
    bov = sb("bov", [128, 8])
    b1v = sb("b1v", [128, 32])
    b2v = sb("b2v", [128, 8])
    hb2 = sb("hb2", [128, 8])
    sclc = sb("sclc", [128, 1])
    mevc = sb("mevc", [128, 1])
    modc = sb("modc", [128, 1])
    u_sb = sb("u_sb", [128, 8])
    epsc = sb("epsc", [1, 1])
    u_res = sb("u_res", [128, 8])
    ucar = sb("ucar", [128, 8])
    ucarS = sb("ucarS", [128, 8])

    psA = [psf(f"psA{i}", [128, 512]) for i in range(4)]
    psD = psf("psD", [128, 1024])
    psS1 = psf("psS1", [128, 512])
    psS2 = psf("psS2", [128, 512])

    names = ("s_x", "s_vec", "s_w1", "s_w2", "s_wo", "s_lt", "s_ccd",
             "s_cc", "s_t", "s_v", "s_s", "s_st", "s_u", "s_sc", "s_sw", "s_g")
    S = {k: Sem(ctx.enter_context(nc.semaphore(k))) for k in names}
    (s_x, s_vec, s_w1, s_w2, s_wo, s_lt, s_ccd, s_cc, s_t, s_v, s_s,
     s_st, s_u, s_sc, s_sw, s_g) = (S[k] for k in names)

    P = Prog()

    def xr(dt):
        return xres[:, dt * T:(dt + 1) * T]

    def ytile(dt):
        return big[:, dt * T:(dt + 1) * T]

    def htile(ht):
        return bigb[:, ht * 1024:(ht + 1) * 1024]

    def ybf(dt, pch):
        return bigb[:, dt * 4096 + pch * 512: dt * 4096 + (pch + 1) * 512]

    def xn(dt):
        return mid[:, dt * 1024:(dt + 1) * 1024]

    R0 = corrt[0:1, 0:512]
    R1 = x2s[:].bitcast(F32)[0:1, 0:512]
    R2 = tmp[0:1, 0:512]

    st = {
        "slot_mm": [0, 0],
        "psS_cons_v": 0,
        "psS_cons_g": 0,
        "psD_cons": [0, 0],
        "psA_cons_s": [0, 0, 0, 0],
        "w1e_cons": [0, 0],
        "w2e_cons": [0, 0],
        "w1_stt2": {},
        "w2_stt": 0,
        "mid_free_t": 0,
        "h_cons_t": 0,
        "big_free_t": 0,
        "tmp_free_v": 0,
        "xn_ready": [0, 0],
        "ep_last_s": 0,
        "bc_last": 0,
        "x_ready": 0,
        "w1_sw": {}, "w2_sw": {},
    }

    for dt in range(NDT):
        P.add("sync", lambda e, dt=dt: e.dma_start(
            out=xr(dt), in_=x_cm[dt * 128:(dt + 1) * 128, :]).then_inc(s_x.h, 16))
        s_x.bump(16)
    for dst, src in ((av, a_pm), (bv, b_pm), (bov, bo_pm), (b1v, b1_pm),
                     (b2v, b2_pm), (sclc, scale_col), (mevc, mask_even),
                     (modc, mask_odd), (ident, ident_d)):
        P.add("sync", lambda e, d=dst, s=src: e.dma_start(out=d[:], in_=s).then_inc(s_vec.h, 16))
        s_vec.bump(16)
    P.wait("vector", s_vec, s_vec.n)
    P.add("vector", lambda e: e.memset(ones_row[:], 1.0))
    P.add("vector", lambda e: e.memset(epsc[:], EPS))
    P.add("vector", lambda e: e.memset(ones_col[:], 1.0))
    P.add("vector", lambda e: e.memset(ones_colf[:], 1.0))
    P.add("vector", lambda e: e.tensor_scalar(out=hb2[:], in0=b2v[:], scalar1=0.5,
                                              scalar2=None, op0=ALU.mult).then_inc(s_v.h, 1))
    s_v.bump()
    prologue_v = s_v.n
    P.wait("scalar", s_vec, s_vec.n)
    P.add("scalar", lambda e: e.activation(ahcol[:], av[:], AF.Tanh))
    P.add("scalar", lambda e: e.activation(ahp512[:], ahcol[:], AF.Abs))
    P.add("scalar", lambda e: e.activation(ahp512[:], ahp512[:], AF.Ln))
    P.add("scalar", lambda e: e.activation(ahp512[:], ahp512[:], AF.Exp,
                                           scale=512.0).then_inc(s_s.h, 1))
    s_s.bump()
    prologue_s = s_s.n
    x_loaded = s_x.n

    def stats_sub(c, s, deferred, xf32=False):
        q = c * CH + s * SUB
        if xf32:
            P.wait("scalar", s_x, x_loaded)
            P.wait("scalar", s_v, st["x_ready"])
            P.wait("scalar", s_v, st["tmp_free_v"])
            P.wait("scalar", s_t, st["bc_last"])
            for dt in range(NDT):
                slot = dt % 2
                P.wait("scalar", s_t, st["slot_mm"][slot])
                P.add("scalar", lambda e, slot=slot, dt=dt, q=q: e.activation(
                    tmp[:, slot * 512:(slot + 1) * 512], xr(dt)[:, q:q + SUB],
                    AF.Square).then_inc(s_s.h, 1))
                s_s.bump()
                P.wait("tensor", s_s, s_s.n)
                if dt == 0:
                    P.wait("tensor", s_g, st["psS_cons_g"])
                    P.wait("tensor", s_v, st["psS_cons_v"])
                P.add("tensor", lambda e, dt=dt, q=q: e.matmul(
                    psS1[0:1, :], ones_colf[:], xr(dt)[:, q:q + SUB],
                    start=(dt == 0), stop=(dt == NDT - 1)))
                P.add("tensor", lambda e, slot=slot, dt=dt: e.matmul(
                    psS2[0:1, :], ones_colf[:], tmp[:, slot * 512:(slot + 1) * 512],
                    start=(dt == 0), stop=(dt == NDT - 1)).then_inc(s_t.h, 1))
                s_t.bump()
                st["slot_mm"][slot] = s_t.n
        else:
            P.wait("gpsimd", s_x, x_loaded)
            P.wait("gpsimd", s_v, st["x_ready"])
            P.wait("gpsimd", s_t, st["bc_last"])
            for dt in range(NDT):
                slot = dt % 2
                P.wait("gpsimd", s_t, st["slot_mm"][slot])
                P.add("gpsimd", lambda e, slot=slot, dt=dt, q=q: e.tensor_copy(
                    xbst[:, slot * 512:(slot + 1) * 512], xr(dt)[:, q:q + SUB]).then_inc(s_g.h, 1))
                s_g.bump()
                P.wait("scalar", s_g, s_g.n)
                P.add("scalar", lambda e, slot=slot: e.activation(
                    x2s[:, slot * 512:(slot + 1) * 512], xbst[:, slot * 512:(slot + 1) * 512],
                    AF.Square).then_inc(s_s.h, 1))
                s_s.bump()
                P.wait("tensor", s_s, s_s.n)
                if dt == 0:
                    P.wait("tensor", s_g, st["psS_cons_g"])
                    P.wait("tensor", s_v, st["psS_cons_v"])
                P.add("tensor", lambda e, slot=slot, dt=dt: e.matmul(
                    psS1[0:1, :], ones_col[:], xbst[:, slot * 512:(slot + 1) * 512],
                    start=(dt == 0), stop=(dt == NDT - 1)))
                P.add("tensor", lambda e, slot=slot, dt=dt: e.matmul(
                    psS2[0:1, :], ones_col[:], x2s[:, slot * 512:(slot + 1) * 512],
                    start=(dt == 0), stop=(dt == NDT - 1)).then_inc(s_t.h, 1))
                s_t.bump()
                st["slot_mm"][slot] = s_t.n
        P.wait("scalar", s_t, s_t.n)
        P.add("scalar", lambda e: e.activation(R0, psS1[0:1, :], AF.Identity, scale=1.0 / D))
        P.add("scalar", lambda e: e.activation(R1, psS2[0:1, :], AF.Identity,
                                               scale=1.0 / D).then_inc(s_s.h, 1))
        s_s.bump()
        P.wait("gpsimd", s_s, s_s.n)
        P.wait("gpsimd", s_v, st["tmp_free_v"])
        P.add("gpsimd", lambda e: e.tensor_tensor(out=R2, in0=R0, in1=R0, op=ALU.mult))
        P.add("gpsimd", lambda e: e.tensor_tensor(out=R1, in0=R1, in1=R2,
                                                  op=ALU.subtract).then_inc(s_g.h, 1))
        s_g.bump()
        st["psS_cons_g"] = s_g.n
        P.wait("scalar", s_g, s_g.n)
        P.add("scalar", lambda e: e.activation(R2, R1, AF.Ln, bias=epsc[0:1, 0:1]))
        P.add("scalar", lambda e: e.activation(R1, R2, AF.Exp, scale=-0.5).then_inc(s_s.h, 1))
        s_s.bump()
        P.wait("gpsimd", s_s, s_s.n)
        P.add("gpsimd", lambda e: e.tensor_tensor(
            out=R2, in0=R0, in1=R1, op=ALU.mult).then_inc(s_g.h, 1))
        s_g.bump()
        rows_g = s_g.n
        tgt_rs = psS1[:] if deferred else psD[:, 0:512]
        tgt_mrs = psS2[:] if deferred else psD[:, 512:1024]
        P.wait("tensor", s_g, rows_g)
        if not deferred:
            P.wait("tensor", s_v, st["psD_cons"][0])
            P.wait("tensor", s_v, st["psD_cons"][1])
        P.add("tensor", lambda e, t=tgt_rs: e.matmul(
            t, ones_row[0:1, :], R1, start=True, stop=True))
        P.add("tensor", lambda e, t=tgt_mrs: e.matmul(
            t, ones_row[0:1, :], R2, start=True, stop=True).then_inc(s_t.h, 1))
        s_t.bump()
        st["bc_last"] = s_t.n
        return s_t.n

    def normalize_sub(c, s, bc_ms, deferred):
        q = c * CH + s * SUB
        rs_bc = psS1[:] if deferred else psD[:, 0:512]
        mrs_bc = psS2[:] if deferred else psD[:, 512:1024]
        P.wait("vector", s_t, bc_ms)
        P.wait("vector", s_t, st["mid_free_t"])
        for dt in range(NDT):
            slot = dt % 2
            P.add("vector", lambda e, dt=dt, slot=slot, q=q, rs=rs_bc: e.tensor_tensor(
                out=ntmp[:, slot * 512:(slot + 1) * 512], in0=xr(dt)[:, q:q + SUB],
                in1=rs, op=ALU.mult))
            ins = lambda e, dt=dt, slot=slot, s=s, mrs=mrs_bc: e.tensor_tensor(
                out=xn(dt)[:, s * 512:(s + 1) * 512],
                in0=ntmp[:, slot * 512:(slot + 1) * 512], in1=mrs, op=ALU.subtract)
            if dt == NDT - 1:
                P.add("vector", lambda e, f=ins: f(e).then_inc(s_v.h, 1))
            else:
                P.add("vector", ins)
        s_v.bump()
        if deferred:
            st["psS_cons_v"] = s_v.n
        else:
            st["psD_cons"][0] = s_v.n
            st["psD_cons"][1] = s_v.n
        return s_v.n

    LT_SLOT = {1: ("B", 0, 0), 2: ("A", 0, 0), 3: ("B", 0, 32), 4: ("A", 0, 32),
               5: ("B", 0, 64), 6: ("A", 0, 64), 7: ("B", 1, 0), 8: ("A", 1, 0),
               9: ("B", 1, 32), 10: ("A", 1, 32), 11: ("B", 1, 64), 12: ("A", 1, 64),
               13: ("B", 2, 0), 14: ("A", 2, 0), 15: ("B", 2, 32), 16: ("A", 2, 32)}

    def lt_pair(k, which):
        if which == "w1":
            ta, tb = 2 * k + 2, 2 * k + 1
        else:
            ta, tb = 10 + 2 * k, 9 + 2 * k
        _, ia, ba = LT_SLOT[ta]
        _, ib, bb = LT_SLOT[tb]
        assert ba == bb
        return ltA[ia][ba:ba + 1, :], ltB[ib][bb:bb + 1, :]

    def lt_phase(call):
        P.wait("tensor", s_v, st["xn_ready"][0])
        for dt in range(NDT):
            pst = (psS1 if dt % 2 == 0 else psS2)[:, 0:64].bitcast(BF16)
            if dt >= 2:
                P.wait("tensor", s_v, st[f"tr_ev_{dt-2}"])
            P.add("tensor", lambda e, dt=dt, pst=pst: e.transpose(
                pst, xn(dt)[:, 0:128], ident[:]).then_inc(s_t.h, 1))
            s_t.bump()
            P.wait("vector", s_t, s_t.n)
            P.add("vector", lambda e, dt=dt, pst=pst: e.tensor_scalar(
                out=contrib[0:32, dt * 128:(dt + 1) * 128], in0=pst[0:32, :],
                scalar1=mevc[0:32, 0:1], scalar2=None, op0=ALU.mult).then_inc(s_v.h, 1))
            s_v.bump()
            st[f"tr_ev_{dt}"] = s_v.n
        st["psS_cons_v"] = s_v.n
        P.wait("gpsimd", s_v, s_v.n)
        P.add("gpsimd", lambda e, call=call: e.dma_start(
            out=cc_lt_in[call].ap(), in_=contrib[1:17, :]).then_inc(s_ccd.h, 16))
        s_ccd.bump(16)
        P.wait("gpsimd", s_ccd, s_ccd.n)
        P.add("gpsimd", lambda e, call=call: e.collective_compute(
            "AllReduce", ALU.add, replica_groups=[[0, 1], [2, 3], [4, 5], [6, 7]],
            ins=[cc_lt_in[call].ap().opt()], outs=[cc_lt_out[call].ap().opt()]).then_inc(s_cc.h, 1))
        s_cc.bump()

    def lt_readback(call):
        P.wait("gpsimd", s_cc, s_cc.n)
        groups = [(ltB[0], 65, 0, 6, 2), (ltA[0], 65, 1, 7, 2),
                  (ltB[1], 65, 6, 11, 2), (ltA[1], 65, 7, 12, 2),
                  (ltB[2], 33, 12, 15, 2), (ltA[2], 33, 13, 16, 2)]
        for tens, pstop, r0, r1, rs in groups:
            P.add("gpsimd", lambda e, tens=tens, pstop=pstop, r0=r0, r1=r1, rs=rs,
                  call=call: e.dma_start(
                out=tens[0:pstop:32, :],
                in_=cc_lt_out[call].ap()[r0:r1:rs, :]).then_inc(s_lt.h, 16))
            s_lt.bump(16)
        return s_lt.n

    def w1_build(ht, lt_ready):
        slot = ht % 2
        k = ht // 8
        i0 = (ht % 8) * 128
        P.wait("gpsimd", s_v, st["w1_stt2"].get(ht - 2, 0))
        P.add("gpsimd", lambda e, ht=ht, slot=slot: e.dma_start(
            out=w1b[:, slot * 1024:(slot + 1) * 1024], in_=w1r[ht]).then_inc(s_w1.h, 16))
        s_w1.bump(16)
        arow, brow = lt_pair(k, "w1")
        P.wait("tensor", s_lt, lt_ready)
        P.wait("tensor", s_v, st["psD_cons"][0])
        P.wait("tensor", s_v, st["psD_cons"][1])
        for dt in range(NDT):
            ins = lambda e, dt=dt, arow=arow, brow=brow, i0=i0: e.matmul(
                psD[:, dt * 128:(dt + 1) * 128], arow[:, dt * 128:(dt + 1) * 128],
                brow[:, i0:i0 + 128], start=True, stop=True)
            if dt == NDT - 1:
                P.add("tensor", lambda e, f=ins: f(e).then_inc(s_t.h, 1))
            else:
                P.add("tensor", ins)
        s_t.bump()
        delta_done = s_t.n
        P.wait("vector", s_t, delta_done)
        P.wait("vector", s_w1, s_w1.n)
        P.wait("vector", s_v, prologue_v)
        P.wait("vector", s_t, st["w1e_cons"][slot])
        P.wait("vector", s_sw, st["w1_sw"].get(ht - 2, 0))
        P.add("vector", lambda e, slot=slot: e.scalar_tensor_tensor(
            out=w1e[:, slot * 1024:(slot + 1) * 1024], in0=psD[:, :],
            scalar=sclc[:, 0:1], in1=w1b[:, slot * 1024:(slot + 1) * 1024],
            op0=ALU.mult, op1=ALU.add).then_inc(s_v.h, 1))
        s_v.bump()
        st["w1_stt2"][ht] = s_v.n
        st["psD_cons"][0] = s_v.n
        st["psD_cons"][1] = s_v.n
        P.wait("gpsimd", s_v, s_v.n)
        P.add("gpsimd", lambda e, ht=ht, slot=slot: e.dma_start(
            out=w1scr[ht], in_=w1e[:, slot * 1024:(slot + 1) * 1024]).then_inc(s_sw.h, 16))
        s_sw.bump(16)
        st["w1_sw"][ht] = s_sw.n
        return s_v.n

    def w1_load(ht):
        slot = ht % 2
        P.wait("sync", s_t, st["w1e_cons"][slot])
        P.wait("sync", s_sw, st["w1_sw"][ht])
        P.add("sync", lambda e, ht=ht, slot=slot: e.dma_start(
            out=w1e[:, slot * 1024:(slot + 1) * 1024], in_=w1scr[ht]).then_inc(s_sc.h, 16))
        s_sc.bump(16)
        return s_sc.n

    def ffn1(c, lt_ready):
        xn_ready = st["xn_ready"][c]
        ready = {}
        for ht in range(NHT + 1):
            if ht < NHT:
                if c == 0:
                    ready[ht] = ("v", w1_build(ht, lt_ready))
                else:
                    ready[ht] = ("sc", w1_load(ht))
            if ht == 0:
                continue
            mt = ht - 1
            slot = mt % 2
            kind, rv = ready.pop(mt)
            pA, pB = psA[slot * 2], psA[slot * 2 + 1]
            P.wait("tensor", s_v if kind == "v" else s_sc, rv)
            P.wait("tensor", s_v, xn_ready)
            P.wait("tensor", s_s, st["psA_cons_s"][slot * 2])
            P.wait("tensor", s_s, st["psA_cons_s"][slot * 2 + 1])
            for dt in range(NDT):
                for half, pp in ((0, pA), (1, pB)):
                    ins = lambda e, dt=dt, slot=slot, pp=pp, half=half: e.matmul(
                        pp[:], w1e[:, slot * 1024 + dt * 128: slot * 1024 + (dt + 1) * 128],
                        xn(dt)[:, half * 512:(half + 1) * 512],
                        start=(dt == 0), stop=(dt == NDT - 1))
                    if dt == NDT - 1 and half == 1:
                        P.add("tensor", lambda e, f=ins: f(e).then_inc(s_t.h, 1))
                    else:
                        P.add("tensor", ins)
            s_t.bump()
            mm_done = s_t.n
            st["w1e_cons"][slot] = mm_done
            if mt == NHT - 1:
                st["mid_free_t"] = mm_done
            P.wait("scalar", s_t, mm_done)
            P.wait("scalar", s_s, prologue_s)
            P.wait("scalar", s_t, st["big_free_t"])
            for half, pp in ((0, pA), (1, pB)):
                ins = lambda e, mt=mt, pp=pp, half=half: e.activation(
                    htile(mt)[:, half * 512:(half + 1) * 512], pp[:], AF.Gelu,
                    bias=b1v[:, mt:mt + 1])
                if half == 1:
                    P.add("scalar", lambda e, f=ins: f(e).then_inc(s_s.h, 1))
                else:
                    P.add("scalar", ins)
            s_s.bump()
            st["psA_cons_s"][slot * 2] = s_s.n
            st["psA_cons_s"][slot * 2 + 1] = s_s.n
        st["h_ready"] = s_s.n

    def w2_build(ot, lt_ready):
        slot = ot % 2
        k = ot // 2
        P.wait("gpsimd", s_v, st["w2_stt"])
        P.add("gpsimd", lambda e, ot=ot: e.dma_start(
            out=w2b[:], in_=w2r[ot]).then_inc(s_w2.h, 16))
        s_w2.bump(16)
        arow, brow = lt_pair(k, "w2")
        P.wait("tensor", s_lt, lt_ready)
        first_stt = True
        for strip in range(8):
            bank = strip % 2
            off = bank * 512
            P.wait("tensor", s_v, st["psD_cons"][bank])
            for j in range(4):
                ht = strip * 4 + j
                cblk = ht // 8
                st0 = 512 * (ot % 2) + cblk
                ins = lambda e, j=j, off=off, arow=arow, brow=brow, st0=st0, ht=ht: e.matmul(
                    psD[:, off + j * 128: off + (j + 1) * 128],
                    arow[:, (ht % 8) * 128:(ht % 8) * 128 + 128],
                    brow[:, st0:st0 + 509:4], start=True, stop=True)
                if j == 3:
                    P.add("tensor", lambda e, f=ins: f(e).then_inc(s_t.h, 1))
                else:
                    P.add("tensor", ins)
            s_t.bump()
            P.wait("vector", s_t, s_t.n)
            P.wait("vector", s_w2, s_w2.n)
            if first_stt:
                P.wait("vector", s_t, st["w2e_cons"][slot])
                P.wait("vector", s_sw, st["w2_sw"].get(ot - 2, 0))
                first_stt = False
            P.add("vector", lambda e, strip=strip, slot=slot, off=off: e.scalar_tensor_tensor(
                out=w2e[:, slot * 4096 + strip * 512: slot * 4096 + (strip + 1) * 512],
                in0=psD[:, off:off + 512], scalar=sclc[:, 0:1],
                in1=w2b[:, strip * 512:(strip + 1) * 512], op0=ALU.mult,
                op1=ALU.add).then_inc(s_v.h, 1))
            s_v.bump()
            st["psD_cons"][bank] = s_v.n
        st["w2_stt"] = s_v.n
        P.wait("gpsimd", s_v, s_v.n)
        P.add("gpsimd", lambda e, ot=ot, slot=slot: e.dma_start(
            out=w2scr[ot], in_=w2e[:, slot * 4096:(slot + 1) * 4096]).then_inc(s_sw.h, 16))
        s_sw.bump(16)
        st["w2_sw"][ot] = s_sw.n
        return s_v.n

    def w2_load(ot):
        slot = ot % 2
        P.wait("sync", s_t, st["w2e_cons"][slot])
        P.wait("sync", s_sw, st["w2_sw"][ot])
        P.add("sync", lambda e, ot=ot, slot=slot: e.dma_start(
            out=w2e[:, slot * 4096:(slot + 1) * 4096], in_=w2scr[ot]).then_inc(s_sc.h, 16))
        s_sc.bump(16)
        return s_sc.n

    def ffn2(c, lt_ready):
        h_ready = st["h_ready"]
        ready = {}
        for ot in range(NOT + 1):
            if ot < NOT:
                if c == 0:
                    ready[ot] = ("v", w2_build(ot, lt_ready))
                else:
                    ready[ot] = ("sc", w2_load(ot))
            if ot == 0:
                continue
            mo = ot - 1
            slot = mo % 2
            kind, rv = ready.pop(mo)
            pA, pB = psA[slot * 2], psA[slot * 2 + 1]
            P.wait("tensor", s_v if kind == "v" else s_sc, rv)
            P.wait("tensor", s_s, h_ready)
            P.wait("tensor", s_s, st["psA_cons_s"][slot * 2])
            P.wait("tensor", s_s, st["psA_cons_s"][slot * 2 + 1])
            for ht in range(NHT):
                for half, pp in ((0, pA), (1, pB)):
                    ins = lambda e, ht=ht, slot=slot, pp=pp, half=half: e.matmul(
                        pp[:], w2e[:, slot * 4096 + ht * 128: slot * 4096 + (ht + 1) * 128],
                        htile(ht)[:, half * 512:(half + 1) * 512],
                        start=(ht == 0), stop=(ht == NHT - 1))
                    if ht == NHT - 1 and half == 1:
                        P.add("tensor", lambda e, f=ins: f(e).then_inc(s_t.h, 1))
                    else:
                        P.add("tensor", ins)
            s_t.bump()
            mm_done = s_t.n
            st["w2e_cons"][slot] = mm_done
            st["h_cons_t"] = mm_done
            P.wait("scalar", s_t, mm_done)
            P.wait("scalar", s_v, st["tmp_free_v"])
            P.wait("scalar", s_v, prologue_v)
            for half, pp in ((0, pA), (1, pB)):
                ins = lambda e, mo=mo, pp=pp, half=half: e.activation(
                    tmp[:, half * 512:(half + 1) * 512], pp[:], AF.Identity,
                    bias=hb2[:, mo:mo + 1], scale=0.5)
                if half == 1:
                    P.add("scalar", lambda e, f=ins: f(e).then_inc(s_s.h, 1))
                else:
                    P.add("scalar", ins)
            s_s.bump()
            st["psA_cons_s"][slot * 2] = s_s.n
            st["psA_cons_s"][slot * 2 + 1] = s_s.n
            st["ep_last_s"] = s_s.n
            P.wait("vector", s_s, s_s.n)
            P.add("vector", lambda e, mo=mo, c=c: e.tensor_tensor(
                out=xr(mo)[:, c * CH:(c + 1) * CH], in0=xr(mo)[:, c * CH:(c + 1) * CH],
                in1=tmp[:, 0:1024], op=ALU.add).then_inc(s_v.h, 1))
            s_v.bump()
            st["tmp_free_v"] = s_v.n
        st["x_ready"] = s_v.n

    def ffn_call(call):
        st["xn_ready"] = [0, 0]
        st["w1_sw"] = {}
        st["w2_sw"] = {}
        st["w1_stt2"] = {}
        for s in range(2):
            bc = stats_sub(0, s, deferred=False)
            st["xn_ready"][0] = normalize_sub(0, s, bc, deferred=False)
        lt_phase(call)
        bc_c1s0 = stats_sub(1, 0, deferred=True)
        lt_ready = lt_readback(call)
        ffn1(0, lt_ready)
        ffn2(0, lt_ready)
        n0 = normalize_sub(1, 0, bc_c1s0, deferred=True)
        P.wait("tensor", s_v, n0)
        bc_c1s1 = stats_sub(1, 1, deferred=True)
        st["xn_ready"][1] = normalize_sub(1, 1, bc_c1s1, deferred=True)
        ffn1(1, lt_ready)
        ffn2(1, lt_ready)

    def attn_phase():
        P.wait("vector", s_s, prologue_s)

        first = True
        norm_ms = {}
        for s4 in range(4):
            c, ss = divmod(s4, 2)
            bc = stats_sub(c, ss, deferred=False, xf32=True)
            rs_bc = psD[:, 0:512]
            q = s4 * SUB
            P.wait("vector", s_t, bc)
            if first:
                P.wait("vector", s_t, st["h_cons_t"])
                P.wait("tensor", s_s, st["psA_cons_s"][0])
                P.wait("tensor", s_s, st["psA_cons_s"][1])
                first = False
            mrs_bc = psD[:, 512:1024]
            for dt in range(NDT):
                nA = ntmp[:, 0:512]
                nB = ntmp[:, 512:1024]
                P.add("vector", lambda e, dt=dt, q=q, rs=rs_bc, nA=nA: e.tensor_tensor(
                    out=nA, in0=xr(dt)[:, q:q + SUB], in1=rs, op=ALU.mult))
                P.add("vector", lambda e, mrs=mrs_bc, nA=nA: e.tensor_tensor(
                    out=nA, in0=nA, in1=mrs, op=ALU.subtract))
                P.add("vector", lambda e, dt=dt, nA=nA, nB=nB: e.tensor_scalar(
                    out=nB, in0=nA, scalar1=bv[:, dt:dt + 1], scalar2=None, op0=ALU.mult))
                if s4 > 0:
                    P.add("vector", lambda e, dt=dt, nB=nB: e.scalar_tensor_tensor(
                        out=nB[:, 0:1], in0=ucarS[:, dt:dt + 1],
                        scalar=ahcol[:, dt:dt + 1], in1=nB[:, 0:1],
                        op0=ALU.mult, op1=ALU.add))
                P.add("vector", lambda e, dt=dt: e.tensor_scalar(
                    out=ahbcT[:], in0=xr(0)[:, 0:512], scalar1=0.0,
                    scalar2=ahcol[:, dt:dt + 1], op0=ALU.mult, op1=ALU.add))
                ins2 = lambda e, dt=dt, q=q, nB=nB: e.tensor_tensor_scan(
                    out=ytile(dt)[:, q:q + SUB], data0=ahbcT[:],
                    data1=nB, initial=0.0, op0=ALU.mult, op1=ALU.add)
                P.add("vector", lambda e, f=ins2: f(e).then_inc(s_v.h, 1))
                s_v.bump()
                norm_ms[dt] = s_v.n
            st["psD_cons"][0] = s_v.n
            st["psD_cons"][1] = s_v.n
            for dt in range(NDT):
                ins3 = lambda e, dt=dt, q=q: e.tensor_copy(
                    ucarS[:, dt:dt + 1], ytile(dt)[:, q + SUB - 1:q + SUB])
                if dt == NDT - 1:
                    P.add("vector", lambda e, f=ins3: f(e).then_inc(s_v.h, 1))
                else:
                    P.add("vector", ins3)
            s_v.bump()
            P.wait("vector", s_v, s_v.n)
        for dt in range(NDT):
            ins = lambda e, dt=dt: e.tensor_scalar(
                out=u_sb[:, dt:dt + 1], in0=ucarS[:, dt:dt + 1],
                scalar1=mevc[:, 0:1], scalar2=None, op0=ALU.mult)
            if dt == NDT - 1:
                P.add("vector", lambda e, f=ins: f(e).then_inc(s_v.h, 1))
            else:
                P.add("vector", ins)
        s_v.bump()
        P.wait("gpsimd", s_v, s_v.n)
        P.add("gpsimd", lambda e: e.dma_start(out=cc_u_in.ap(), in_=u_sb[:]).then_inc(s_u.h, 16))
        s_u.bump(16)
        P.wait("gpsimd", s_u, s_u.n)
        P.add("gpsimd", lambda e: e.collective_compute(
            "AllReduce", ALU.add, replica_groups=[[0, 1], [2, 3], [4, 5], [6, 7]],
            ins=[cc_u_in.ap().opt()], outs=[cc_u_out.ap().opt()]).then_inc(s_cc.h, 1))
        s_cc.bump()
        zeros = x2s[:].bitcast(F32)[:, 0:512]
        P.wait("vector", s_t, st["mid_free_t"])
        P.add("vector", lambda e: e.memset(zeros, 0.0))
        for dt in range(NDT):
            P.add("vector", lambda e, dt=dt: e.tensor_scalar(
                out=ahbcT[:], in0=xr(0)[:, 0:512], scalar1=0.0,
                scalar2=ahcol[:, dt:dt + 1], op0=ALU.mult, op1=ALU.add))
            ins = lambda e, dt=dt: e.tensor_tensor_scan(
                out=mid[:, dt * 512:(dt + 1) * 512], data0=ahbcT[:], data1=zeros,
                initial=1.0, op0=ALU.mult, op1=ALU.add)
            if dt == NDT - 1:
                P.add("vector", lambda e, f=ins: f(e).then_inc(s_v.h, 1))
            else:
                P.add("vector", ins)
        s_v.bump()
        pow_ready = s_v.n
        P.wait("gpsimd", s_t, st["w2e_cons"][0])
        P.wait("gpsimd", s_t, st["w2e_cons"][1])
        for et in range(NET):
            P.add("gpsimd", lambda e, et=et: e.dma_start(
                out=w2e[:, et * 1024:(et + 1) * 1024], in_=wor[et]).then_inc(s_wo.h, 16))
            s_wo.bump(16)
        wo_ld = s_wo.n
        P.wait("gpsimd", s_cc, s_cc.n)
        P.add("gpsimd", lambda e: e.dma_start(out=u_res[:], in_=cc_u_out.ap()).then_inc(s_u.h, 16))
        s_u.bump(16)
        P.wait("vector", s_u, s_u.n)
        P.add("vector", lambda e: e.tensor_scalar(
            out=u_res[:], in0=u_res[:], scalar1=modc[:, 0:1], scalar2=None,
            op0=ALU.mult).then_inc(s_v.h, 1))
        s_v.bump()
        P.wait("vector", s_v, s_v.n)
        ybf_ms = []
        for pch in range(4):
            q = pch * SUB
            for dt in range(NDT):
                initc = u_res[:, dt:dt + 1] if pch == 0 else ucar[:, dt:dt + 1]
                ins = lambda e, dt=dt, q=q, pch=pch, initc=initc: e.scalar_tensor_tensor(
                    out=ybf(dt, pch), in0=mid[:, dt * 512:(dt + 1) * 512],
                    scalar=initc, in1=ytile(dt)[:, q:q + SUB],
                    op0=ALU.mult, op1=ALU.add)
                if dt == NDT - 1:
                    P.add("vector", lambda e, f=ins: f(e).then_inc(s_v.h, 1))
                else:
                    P.add("vector", ins)
                P.add("vector", lambda e, dt=dt, initc=initc: e.tensor_scalar(
                    out=ucar[:, dt:dt + 1], in0=initc, scalar1=ahp512[:, dt:dt + 1],
                    scalar2=None, op0=ALU.mult))
            s_v.bump()
            ybf_ms.append(s_v.n)
            P.wait("vector", s_v, s_v.n)
        proj_started = False
        last_mm = 0
        for pch in range(4):
            for et in range(NET):
                bank = et % 4
                pp = psA[bank]
                P.wait("tensor", s_wo, wo_ld)
                P.wait("tensor", s_v, ybf_ms[pch])
                if not proj_started:
                    P.wait("tensor", s_s, st["ep_last_s"])
                    P.wait("tensor", s_v, pow_ready)
                    proj_started = True
                P.wait("tensor", s_s, st["psA_cons_s"][bank])
                for dt in range(NDT):
                    ins = lambda e, dt=dt, et=et, pp=pp, pch=pch: e.matmul(
                        pp[:], w2e[:, et * 1024 + dt * 128: et * 1024 + (dt + 1) * 128],
                        ybf(dt, pch), start=(dt == 0), stop=(dt == NDT - 1))
                    if dt == NDT - 1:
                        P.add("tensor", lambda e, f=ins: f(e).then_inc(s_t.h, 1))
                    else:
                        P.add("tensor", ins)
                s_t.bump()
                mm_done = s_t.n
                last_mm = mm_done
                P.wait("scalar", s_t, mm_done)
                P.wait("scalar", s_v, st["tmp_free_v"])
                P.add("scalar", lambda e, et=et, pp=pp: e.activation(
                    tmp[:, 0:512], pp[:], AF.Identity,
                    bias=bov[:, et:et + 1]).then_inc(s_s.h, 1))
                s_s.bump()
                st["psA_cons_s"][bank] = s_s.n
                P.wait("vector", s_s, s_s.n)
                P.add("vector", lambda e, et=et, pch=pch: e.tensor_tensor(
                    out=xr(et)[:, pch * SUB:(pch + 1) * SUB],
                    in0=xr(et)[:, pch * SUB:(pch + 1) * SUB],
                    in1=tmp[:, 0:512], op=ALU.add).then_inc(s_v.h, 1))
                s_v.bump()
                st["tmp_free_v"] = s_v.n
        st["big_free_t"] = last_mm
        st["w2e_cons"] = [last_mm, last_mm]
        st["mid_free_t"] = 0
        st["x_ready"] = s_v.n

    import os
    stop = os.environ.get("KSTOP", "")
    ffn_call(0)
    if stop != "ffn0":
        attn_phase()
        if stop != "attn":
            ffn_call(1)

    P.wait("sync", s_v, st["x_ready"])
    for dt in range(NDT):
        P.add("sync", lambda e, dt=dt: e.dma_start(
            out=out_cm[dt * 128:(dt + 1) * 128, :], in_=xr(dt)).then_inc(s_st.h, 16))
        s_st.bump(16)
    P.wait("sync", s_st, s_st.n)

    with ctx:
        with nc.Block() as block:
            @block.sync
            def _(e):
                for fn in P.q["sync"]:
                    fn(e)

            @block.gpsimd
            def _(e):
                for fn in P.q["gpsimd"]:
                    fn(e)

            @block.vector
            def _(e):
                for fn in P.q["vector"]:
                    fn(e)

            @block.scalar
            def _(e):
                for fn in P.q["scalar"]:
                    fn(e)

            @block.tensor
            def _(e):
                for fn in P.q["tensor"]:
                    fn(e)

    return nc


def kernel(L0_attn_x, L0_attn_a, L0_attn_b, L0_attn_w_out, L0_attn_b_out,
           L0_ffn_W1, L0_ffn_b1, L0_ffn_W2, L0_ffn_b2, L0_ffn_scale):
    x = np.asarray(L0_attn_x, dtype=np.float32)
    W1 = np.asarray(L0_ffn_W1, np.float32)
    W2 = np.asarray(L0_ffn_W2, np.float32)
    wo = np.asarray(L0_attn_w_out, np.float32)
    a = np.asarray(L0_attn_a, np.float32)
    b = np.asarray(L0_attn_b, np.float32)
    b_out = np.asarray(L0_attn_b_out, np.float32)
    b1 = np.asarray(L0_ffn_b1, np.float32)
    b2 = np.asarray(L0_ffn_b2, np.float32)
    scale = np.float32(np.asarray(L0_ffn_scale).reshape(()))

    w1r = np.ascontiguousarray(np.transpose(W1.reshape(32, 128, 8, 128), (0, 3, 2, 1)))
    w2r = np.ascontiguousarray(np.transpose(W2.reshape(8, 128, 32, 128), (0, 3, 2, 1)))
    wor = np.ascontiguousarray(np.transpose(wo.reshape(8, 128, 8, 128), (0, 3, 2, 1)))
    a_pm = np.ascontiguousarray(a.reshape(8, 128).T)
    b_pm = np.ascontiguousarray(b.reshape(8, 128).T)
    bo_pm = np.ascontiguousarray(b_out.reshape(8, 128).T)
    b1_pm = np.ascontiguousarray(b1.reshape(32, 128).T)
    b2_pm = np.ascontiguousarray(b2.reshape(8, 128).T)
    scale_col = np.full((128, 1), scale, np.float32)
    ident = np.eye(128, dtype=ml_dtypes.bfloat16)

    nc = build_program()
    in_maps = []
    for core in range(8):
        bidx, half = divmod(core, 2)
        xs = np.ascontiguousarray(x[bidx, half * T:(half + 1) * T, :].T)
        in_maps.append({
            "x_cm": xs, "w1r": w1r, "w2r": w2r, "wor": wor,
            "a_pm": a_pm, "b_pm": b_pm, "bo_pm": bo_pm, "b1_pm": b1_pm,
            "b2_pm": b2_pm, "scale_col": scale_col,
            "mask_even": np.full((128, 1), 1.0 if half == 0 else 0.0, np.float32),
            "mask_odd": np.full((128, 1), 1.0 if half == 1 else 0.0, np.float32),
            "ident_d": ident,
        })
    res = run_bass_kernel_spmd(nc, in_maps, core_ids=list(range(8)))
    out = np.empty((4, 4096, 1024), np.float32)
    for core in range(8):
        bidx, half = divmod(core, 2)
        out[bidx, half * T:(half + 1) * T, :] = res.results[core]["out_cm"].T
    return out


# version 08
# speedup vs baseline: 1.1507x; 1.1507x over previous
import numpy as np
import ml_dtypes
from contextlib import ExitStack
import concourse.bass as bass
import concourse.mybir as mybir
from concourse.bass_utils import run_bass_kernel_spmd

F32 = mybir.dt.float32
BF16 = mybir.dt.bfloat16
AF = mybir.ActivationFunctionType
ALU = mybir.AluOpType

D = 1024
T = 2048
CH = 1024
SUB = 512
NDT = 8
NHT = 32
NOT = 8
NET = 8
EPS = 1e-5


class Sem:
    def __init__(self, h):
        self.h = h
        self.n = 0

    def bump(self, k=1):
        self.n += k
        return self.n


class Prog:
    ENGINES = ("sync", "gpsimd", "vector", "scalar", "tensor")

    def __init__(self):
        self.q = {e: [] for e in self.ENGINES}

    def add(self, eng, fn):
        self.q[eng].append(fn)

    def wait(self, eng, sem, val):
        if val > 0:
            self.q[eng].append(lambda e, s=sem, v=val: e.wait_ge(s.h, v))


def build_program():
    nc = bass.Bass("TRN2", target_bir_lowering=False)

    x_cm = nc.dram_tensor("x_cm", [D, T], F32, kind="ExternalInput").ap()
    w1r = nc.dram_tensor("w1r", [NHT, 128, 1024], F32, kind="ExternalInput").ap()
    w2r = nc.dram_tensor("w2r", [NOT, 128, 4096], F32, kind="ExternalInput").ap()
    wor = nc.dram_tensor("wor", [NET, 128, 1024], F32, kind="ExternalInput").ap()
    a_pm = nc.dram_tensor("a_pm", [128, 8], F32, kind="ExternalInput").ap()
    b_pm = nc.dram_tensor("b_pm", [128, 8], F32, kind="ExternalInput").ap()
    bo_pm = nc.dram_tensor("bo_pm", [128, 8], F32, kind="ExternalInput").ap()
    b1_pm = nc.dram_tensor("b1_pm", [128, 32], F32, kind="ExternalInput").ap()
    b2_pm = nc.dram_tensor("b2_pm", [128, 8], F32, kind="ExternalInput").ap()
    scale_col = nc.dram_tensor("scale_col", [128, 1], F32, kind="ExternalInput").ap()
    mask_even = nc.dram_tensor("mask_even", [128, 1], F32, kind="ExternalInput").ap()
    mask_odd = nc.dram_tensor("mask_odd", [128, 1], F32, kind="ExternalInput").ap()
    ident_d = nc.dram_tensor("ident_d", [128, 128], BF16, kind="ExternalInput").ap()
    out_cm = nc.dram_tensor("out_cm", [D, T], F32, kind="ExternalOutput").ap()

    cc_lt_in = [nc.dram_tensor(f"cc_lt{i}_in", [16, 1024], BF16) for i in range(2)]
    cc_lt_out = [nc.dram_tensor(f"cc_lt{i}_out", [16, 1024], BF16) for i in range(2)]
    cc_u_in = nc.dram_tensor("cc_u_in", [128, 8], F32)
    cc_u_out = nc.dram_tensor("cc_u_out", [128, 8], F32)
    w1scr = nc.dram_tensor("w1scr", [NHT, 128, 1024], BF16).ap()
    w2scr = nc.dram_tensor("w2scr", [NOT, 128, 4096], BF16).ap()

    ctx = ExitStack()
    sb = lambda n, s, d=F32: ctx.enter_context(nc.sbuf_tensor(n, s, d))
    psf = lambda n, s: ctx.enter_context(nc.psum_tensor(n, s, F32))

    xres = sb("xres", [128, NDT * T])
    big = sb("big", [128, 16384])
    bigb = big[:].bitcast(BF16)
    mid = sb("mid", [128, 8192], BF16)
    xbst = sb("xbst", [128, 1024], BF16)
    x2s = sb("x2s", [128, 1024], BF16)
    ntmp = sb("ntmp", [128, 1024])
    tmp = sb("tmp", [128, 1024])
    w1b = sb("w1b", [128, 2048], BF16)
    w1e = sb("w1e", [128, 2048], BF16)
    w2b = sb("w2b", [128, 4096], BF16)
    w2e = sb("w2e", [128, 8192], BF16)
    ltA = [sb(f"ltA{i}", [65, 1024], BF16) for i in range(3)]
    ltB = [sb(f"ltB{i}", [65, 1024], BF16) for i in range(3)]
    contrib = sb("contrib", [32, 1024], BF16)
    corrt = sb("corrt", [128, 512])
    ident = sb("ident", [128, 128], BF16)
    ones_row = sb("ones_row", [1, 128])
    ones_col = sb("ones_col", [128, 1], BF16)
    ones_colf = sb("ones_colf", [128, 1])
    ahbcT = sb("ahbcT", [128, 512])
    av = sb("av", [128, 8])
    ahcol = sb("ahcol", [128, 8])
    ahp512 = sb("ahp512", [128, 8])
    bv = sb("bv", [128, 8])
    bov = sb("bov", [128, 8])
    b1v = sb("b1v", [128, 32])
    b2v = sb("b2v", [128, 8])
    hb2 = sb("hb2", [128, 8])
    sclc = sb("sclc", [128, 1])
    mevc = sb("mevc", [128, 1])
    modc = sb("modc", [128, 1])
    u_sb = sb("u_sb", [128, 8])
    epsc = sb("epsc", [1, 1])
    u_res = sb("u_res", [128, 8])
    ucar = sb("ucar", [128, 8])
    ucarS = sb("ucarS", [128, 8])

    psA = [psf(f"psA{i}", [128, 512]) for i in range(4)]
    psD = psf("psD", [128, 1024])
    psS1 = psf("psS1", [128, 512])
    psS2 = psf("psS2", [128, 512])

    names = ("s_x", "s_vec", "s_w1", "s_w2", "s_wo", "s_lt", "s_ccd",
             "s_cc", "s_t", "s_v", "s_s", "s_st", "s_u", "s_sc", "s_sw", "s_g")
    S = {k: Sem(ctx.enter_context(nc.semaphore(k))) for k in names}
    (s_x, s_vec, s_w1, s_w2, s_wo, s_lt, s_ccd, s_cc, s_t, s_v, s_s,
     s_st, s_u, s_sc, s_sw, s_g) = (S[k] for k in names)

    P = Prog()

    def xr(dt):
        return xres[:, dt * T:(dt + 1) * T]

    def ytile(dt):
        return big[:, dt * T:(dt + 1) * T]

    def htile(ht):
        return bigb[:, ht * 1024:(ht + 1) * 1024]

    def ybf(dt, pch):
        return bigb[:, dt * 4096 + pch * 512: dt * 4096 + (pch + 1) * 512]

    def xn(dt):
        return mid[:, dt * 1024:(dt + 1) * 1024]

    R0 = corrt[0:1, 0:512]
    R1 = x2s[:].bitcast(F32)[0:1, 0:512]
    R2 = tmp[0:1, 0:512]

    st = {
        "slot_mm": [0, 0],
        "slot_mm4": [0, 0, 0, 0],
        "psS_cons_v": 0,
        "psS_cons_g": 0,
        "psD_cons": [0, 0],
        "psA_cons_s": [0, 0, 0, 0],
        "w1e_cons": [0, 0],
        "w2e_cons": [0, 0],
        "w1_stt2": {},
        "w2_stt": 0,
        "mid_free_t": 0,
        "h_cons_t": 0,
        "big_free_t": 0,
        "tmp_free_v": 0,
        "xn_ready": [0, 0],
        "ep_last_s": 0,
        "bc_last": 0,
        "x_ready": 0,
        "w1_sw": {}, "w2_sw": {},
    }

    for dt in range(NDT):
        P.add("sync", lambda e, dt=dt: e.dma_start(
            out=xr(dt), in_=x_cm[dt * 128:(dt + 1) * 128, :]).then_inc(s_x.h, 16))
        s_x.bump(16)
    for dst, src in ((av, a_pm), (bv, b_pm), (bov, bo_pm), (b1v, b1_pm),
                     (b2v, b2_pm), (sclc, scale_col), (mevc, mask_even),
                     (modc, mask_odd), (ident, ident_d)):
        P.add("sync", lambda e, d=dst, s=src: e.dma_start(out=d[:], in_=s).then_inc(s_vec.h, 16))
        s_vec.bump(16)
    P.wait("vector", s_vec, s_vec.n)
    P.add("vector", lambda e: e.memset(ones_row[:], 1.0))
    P.add("vector", lambda e: e.memset(epsc[:], EPS))
    P.add("vector", lambda e: e.memset(ones_col[:], 1.0))
    P.add("vector", lambda e: e.memset(ones_colf[:], 1.0))
    P.add("vector", lambda e: e.tensor_scalar(out=hb2[:], in0=b2v[:], scalar1=0.5,
                                              scalar2=None, op0=ALU.mult).then_inc(s_v.h, 1))
    s_v.bump()
    prologue_v = s_v.n
    P.wait("scalar", s_vec, s_vec.n)
    P.add("scalar", lambda e: e.activation(ahcol[:], av[:], AF.Tanh))
    P.add("scalar", lambda e: e.activation(ahp512[:], ahcol[:], AF.Abs))
    P.add("scalar", lambda e: e.activation(ahp512[:], ahp512[:], AF.Ln))
    P.add("scalar", lambda e: e.activation(ahp512[:], ahp512[:], AF.Exp,
                                           scale=512.0).then_inc(s_s.h, 1))
    s_s.bump()
    prologue_s = s_s.n
    x_loaded = s_x.n

    def stats_sub(c, s, deferred, xf32=False):
        q = c * CH + s * SUB
        if xf32:
            P.wait("scalar", s_x, x_loaded)
            P.wait("scalar", s_v, st["x_ready"])
            P.wait("scalar", s_v, st["tmp_free_v"])
            P.wait("scalar", s_t, st["bc_last"])
            for dt in range(NDT):
                slot = dt % 2
                P.wait("scalar", s_t, st["slot_mm"][slot])
                P.add("scalar", lambda e, slot=slot, dt=dt, q=q: e.activation(
                    tmp[:, slot * 512:(slot + 1) * 512], xr(dt)[:, q:q + SUB],
                    AF.Square).then_inc(s_s.h, 1))
                s_s.bump()
                P.wait("tensor", s_s, s_s.n)
                if dt == 0:
                    P.wait("tensor", s_g, st["psS_cons_g"])
                    P.wait("tensor", s_v, st["psS_cons_v"])
                P.add("tensor", lambda e, dt=dt, q=q: e.matmul(
                    psS1[0:1, :], ones_colf[:], xr(dt)[:, q:q + SUB],
                    start=(dt == 0), stop=(dt == NDT - 1)))
                P.add("tensor", lambda e, slot=slot, dt=dt: e.matmul(
                    psS2[0:1, :], ones_colf[:], tmp[:, slot * 512:(slot + 1) * 512],
                    start=(dt == 0), stop=(dt == NDT - 1)).then_inc(s_t.h, 1))
                s_t.bump()
                st["slot_mm"][slot] = s_t.n
        else:
            P.wait("gpsimd", s_x, x_loaded)
            P.wait("gpsimd", s_v, st["x_ready"])
            P.wait("gpsimd", s_v, st["tmp_free_v"])
            P.wait("gpsimd", s_t, st["bc_last"])
            tmpb = tmp[:].bitcast(BF16)
            xslot = lambda i: (xbst if i < 2 else x2s)[:, (i % 2) * 512:(i % 2 + 1) * 512]
            x2slot = lambda i: tmpb[:, i * 512:(i + 1) * 512]
            for dt in range(NDT):
                slot = dt % 4
                P.wait("gpsimd", s_t, st["slot_mm4"][slot])
                P.add("gpsimd", lambda e, slot=slot, dt=dt, q=q: e.tensor_copy(
                    xslot(slot), xr(dt)[:, q:q + SUB]).then_inc(s_g.h, 1))
                s_g.bump()
                P.wait("scalar", s_g, s_g.n)
                P.add("scalar", lambda e, slot=slot: e.activation(
                    x2slot(slot), xslot(slot), AF.Square).then_inc(s_s.h, 1))
                s_s.bump()
                P.wait("tensor", s_s, s_s.n)
                if dt == 0:
                    P.wait("tensor", s_g, st["psS_cons_g"])
                    P.wait("tensor", s_v, st["psS_cons_v"])
                P.add("tensor", lambda e, slot=slot, dt=dt: e.matmul(
                    psS1[0:1, :], ones_col[:], xslot(slot),
                    start=(dt == 0), stop=(dt == NDT - 1)))
                P.add("tensor", lambda e, slot=slot, dt=dt: e.matmul(
                    psS2[0:1, :], ones_col[:], x2slot(slot),
                    start=(dt == 0), stop=(dt == NDT - 1)).then_inc(s_t.h, 1))
                s_t.bump()
                st["slot_mm4"][slot] = s_t.n
        P.wait("scalar", s_t, s_t.n)
        P.add("scalar", lambda e: e.activation(R0, psS1[0:1, :], AF.Identity, scale=1.0 / D))
        P.add("scalar", lambda e: e.activation(R1, psS2[0:1, :], AF.Identity,
                                               scale=1.0 / D).then_inc(s_s.h, 1))
        s_s.bump()
        P.wait("gpsimd", s_s, s_s.n)
        P.wait("gpsimd", s_v, st["tmp_free_v"])
        P.add("gpsimd", lambda e: e.tensor_tensor(out=R2, in0=R0, in1=R0, op=ALU.mult))
        P.add("gpsimd", lambda e: e.tensor_tensor(out=R1, in0=R1, in1=R2,
                                                  op=ALU.subtract).then_inc(s_g.h, 1))
        s_g.bump()
        st["psS_cons_g"] = s_g.n
        P.wait("scalar", s_g, s_g.n)
        P.add("scalar", lambda e: e.activation(R2, R1, AF.Ln, bias=epsc[0:1, 0:1]))
        P.add("scalar", lambda e: e.activation(R1, R2, AF.Exp, scale=-0.5).then_inc(s_s.h, 1))
        s_s.bump()
        P.wait("gpsimd", s_s, s_s.n)
        P.add("gpsimd", lambda e: e.tensor_tensor(
            out=R2, in0=R0, in1=R1, op=ALU.mult).then_inc(s_g.h, 1))
        s_g.bump()
        rows_g = s_g.n
        tgt_rs = psS1[:] if deferred else psD[:, 0:512]
        tgt_mrs = psS2[:] if deferred else psD[:, 512:1024]
        P.wait("tensor", s_g, rows_g)
        if not deferred:
            P.wait("tensor", s_v, st["psD_cons"][0])
            P.wait("tensor", s_v, st["psD_cons"][1])
        P.add("tensor", lambda e, t=tgt_rs: e.matmul(
            t, ones_row[0:1, :], R1, start=True, stop=True))
        P.add("tensor", lambda e, t=tgt_mrs: e.matmul(
            t, ones_row[0:1, :], R2, start=True, stop=True).then_inc(s_t.h, 1))
        s_t.bump()
        st["bc_last"] = s_t.n
        return s_t.n

    def normalize_sub(c, s, bc_ms, deferred):
        q = c * CH + s * SUB
        rs_bc = psS1[:] if deferred else psD[:, 0:512]
        mrs_bc = psS2[:] if deferred else psD[:, 512:1024]
        P.wait("vector", s_t, bc_ms)
        P.wait("vector", s_t, st["mid_free_t"])
        for dt in range(NDT):
            slot = dt % 2
            P.add("vector", lambda e, dt=dt, slot=slot, q=q, rs=rs_bc: e.tensor_tensor(
                out=ntmp[:, slot * 512:(slot + 1) * 512], in0=xr(dt)[:, q:q + SUB],
                in1=rs, op=ALU.mult))
            ins = lambda e, dt=dt, slot=slot, s=s, mrs=mrs_bc: e.tensor_tensor(
                out=xn(dt)[:, s * 512:(s + 1) * 512],
                in0=ntmp[:, slot * 512:(slot + 1) * 512], in1=mrs, op=ALU.subtract)
            if dt == NDT - 1:
                P.add("vector", lambda e, f=ins: f(e).then_inc(s_v.h, 1))
            else:
                P.add("vector", ins)
        s_v.bump()
        if deferred:
            st["psS_cons_v"] = s_v.n
        else:
            st["psD_cons"][0] = s_v.n
            st["psD_cons"][1] = s_v.n
        return s_v.n

    LT_SLOT = {1: ("B", 0, 0), 2: ("A", 0, 0), 3: ("B", 0, 32), 4: ("A", 0, 32),
               5: ("B", 0, 64), 6: ("A", 0, 64), 7: ("B", 1, 0), 8: ("A", 1, 0),
               9: ("B", 1, 32), 10: ("A", 1, 32), 11: ("B", 1, 64), 12: ("A", 1, 64),
               13: ("B", 2, 0), 14: ("A", 2, 0), 15: ("B", 2, 32), 16: ("A", 2, 32)}

    def lt_pair(k, which):
        if which == "w1":
            ta, tb = 2 * k + 2, 2 * k + 1
        else:
            ta, tb = 10 + 2 * k, 9 + 2 * k
        _, ia, ba = LT_SLOT[ta]
        _, ib, bb = LT_SLOT[tb]
        assert ba == bb
        return ltA[ia][ba:ba + 1, :], ltB[ib][bb:bb + 1, :]

    def lt_phase(call):
        P.wait("tensor", s_v, st["xn_ready"][0])
        for dt in range(NDT):
            pst = (psS1 if dt % 2 == 0 else psS2)[:, 0:64].bitcast(BF16)
            if dt >= 2:
                P.wait("tensor", s_v, st[f"tr_ev_{dt-2}"])
            P.add("tensor", lambda e, dt=dt, pst=pst: e.transpose(
                pst, xn(dt)[:, 0:128], ident[:]).then_inc(s_t.h, 1))
            s_t.bump()
            P.wait("vector", s_t, s_t.n)
            P.add("vector", lambda e, dt=dt, pst=pst: e.tensor_scalar(
                out=contrib[0:32, dt * 128:(dt + 1) * 128], in0=pst[0:32, :],
                scalar1=mevc[0:32, 0:1], scalar2=None, op0=ALU.mult).then_inc(s_v.h, 1))
            s_v.bump()
            st[f"tr_ev_{dt}"] = s_v.n
        st["psS_cons_v"] = s_v.n
        P.wait("gpsimd", s_v, s_v.n)
        P.add("gpsimd", lambda e, call=call: e.dma_start(
            out=cc_lt_in[call].ap(), in_=contrib[1:17, :]).then_inc(s_ccd.h, 16))
        s_ccd.bump(16)
        P.wait("gpsimd", s_ccd, s_ccd.n)
        P.add("gpsimd", lambda e, call=call: e.collective_compute(
            "AllReduce", ALU.add, replica_groups=[[0, 1], [2, 3], [4, 5], [6, 7]],
            ins=[cc_lt_in[call].ap().opt()], outs=[cc_lt_out[call].ap().opt()]).then_inc(s_cc.h, 1))
        s_cc.bump()

    def lt_readback(call):
        P.wait("gpsimd", s_cc, s_cc.n)
        groups = [(ltB[0], 65, 0, 6, 2), (ltA[0], 65, 1, 7, 2),
                  (ltB[1], 65, 6, 11, 2), (ltA[1], 65, 7, 12, 2),
                  (ltB[2], 33, 12, 15, 2), (ltA[2], 33, 13, 16, 2)]
        for tens, pstop, r0, r1, rs in groups:
            P.add("gpsimd", lambda e, tens=tens, pstop=pstop, r0=r0, r1=r1, rs=rs,
                  call=call: e.dma_start(
                out=tens[0:pstop:32, :],
                in_=cc_lt_out[call].ap()[r0:r1:rs, :]).then_inc(s_lt.h, 16))
            s_lt.bump(16)
        return s_lt.n

    def w1_build(ht, lt_ready):
        slot = ht % 2
        k = ht // 8
        i0 = (ht % 8) * 128
        P.wait("gpsimd", s_v, st["w1_stt2"].get(ht - 2, 0))
        P.add("gpsimd", lambda e, ht=ht, slot=slot: e.dma_start(
            out=w1b[:, slot * 1024:(slot + 1) * 1024], in_=w1r[ht]).then_inc(s_w1.h, 16))
        s_w1.bump(16)
        arow, brow = lt_pair(k, "w1")
        P.wait("tensor", s_lt, lt_ready)
        P.wait("tensor", s_v, st["psD_cons"][0])
        P.wait("tensor", s_v, st["psD_cons"][1])
        for dt in range(NDT):
            ins = lambda e, dt=dt, arow=arow, brow=brow, i0=i0: e.matmul(
                psD[:, dt * 128:(dt + 1) * 128], arow[:, dt * 128:(dt + 1) * 128],
                brow[:, i0:i0 + 128], start=True, stop=True)
            if dt == NDT - 1:
                P.add("tensor", lambda e, f=ins: f(e).then_inc(s_t.h, 1))
            else:
                P.add("tensor", ins)
        s_t.bump()
        delta_done = s_t.n
        P.wait("vector", s_t, delta_done)
        P.wait("vector", s_w1, s_w1.n)
        P.wait("vector", s_v, prologue_v)
        P.wait("vector", s_t, st["w1e_cons"][slot])
        P.wait("vector", s_sw, st["w1_sw"].get(ht - 2, 0))
        P.add("vector", lambda e, slot=slot: e.scalar_tensor_tensor(
            out=w1e[:, slot * 1024:(slot + 1) * 1024], in0=psD[:, :],
            scalar=sclc[:, 0:1], in1=w1b[:, slot * 1024:(slot + 1) * 1024],
            op0=ALU.mult, op1=ALU.add).then_inc(s_v.h, 1))
        s_v.bump()
        st["w1_stt2"][ht] = s_v.n
        st["psD_cons"][0] = s_v.n
        st["psD_cons"][1] = s_v.n
        P.wait("gpsimd", s_v, s_v.n)
        P.add("gpsimd", lambda e, ht=ht, slot=slot: e.dma_start(
            out=w1scr[ht], in_=w1e[:, slot * 1024:(slot + 1) * 1024]).then_inc(s_sw.h, 16))
        s_sw.bump(16)
        st["w1_sw"][ht] = s_sw.n
        return s_v.n

    def w1_load(ht):
        slot = ht % 2
        P.wait("sync", s_t, st["w1e_cons"][slot])
        P.wait("sync", s_sw, st["w1_sw"][ht])
        P.add("sync", lambda e, ht=ht, slot=slot: e.dma_start(
            out=w1e[:, slot * 1024:(slot + 1) * 1024], in_=w1scr[ht]).then_inc(s_sc.h, 16))
        s_sc.bump(16)
        return s_sc.n

    def ffn1(c, lt_ready):
        xn_ready = st["xn_ready"][c]
        ready = {}
        for ht in range(NHT + 1):
            if ht < NHT:
                if c == 0:
                    ready[ht] = ("v", w1_build(ht, lt_ready))
                else:
                    ready[ht] = ("sc", w1_load(ht))
            if ht == 0:
                continue
            mt = ht - 1
            slot = mt % 2
            kind, rv = ready.pop(mt)
            pA, pB = psA[slot * 2], psA[slot * 2 + 1]
            P.wait("tensor", s_v if kind == "v" else s_sc, rv)
            P.wait("tensor", s_v, xn_ready)
            P.wait("tensor", s_s, st["psA_cons_s"][slot * 2])
            P.wait("tensor", s_s, st["psA_cons_s"][slot * 2 + 1])
            for dt in range(NDT):
                for half, pp in ((0, pA), (1, pB)):
                    ins = lambda e, dt=dt, slot=slot, pp=pp, half=half: e.matmul(
                        pp[:], w1e[:, slot * 1024 + dt * 128: slot * 1024 + (dt + 1) * 128],
                        xn(dt)[:, half * 512:(half + 1) * 512],
                        start=(dt == 0), stop=(dt == NDT - 1))
                    if dt == NDT - 1 and half == 1:
                        P.add("tensor", lambda e, f=ins: f(e).then_inc(s_t.h, 1))
                    else:
                        P.add("tensor", ins)
            s_t.bump()
            mm_done = s_t.n
            st["w1e_cons"][slot] = mm_done
            if mt == NHT - 1:
                st["mid_free_t"] = mm_done
            P.wait("scalar", s_t, mm_done)
            P.wait("scalar", s_s, prologue_s)
            P.wait("scalar", s_t, st["big_free_t"])
            for half, pp in ((0, pA), (1, pB)):
                ins = lambda e, mt=mt, pp=pp, half=half: e.activation(
                    htile(mt)[:, half * 512:(half + 1) * 512], pp[:], AF.Gelu,
                    bias=b1v[:, mt:mt + 1])
                if half == 1:
                    P.add("scalar", lambda e, f=ins: f(e).then_inc(s_s.h, 1))
                else:
                    P.add("scalar", ins)
            s_s.bump()
            st["psA_cons_s"][slot * 2] = s_s.n
            st["psA_cons_s"][slot * 2 + 1] = s_s.n
        st["h_ready"] = s_s.n

    def w2_build(ot, lt_ready):
        slot = ot % 2
        k = ot // 2
        P.wait("gpsimd", s_v, st["w2_stt"])
        P.add("gpsimd", lambda e, ot=ot: e.dma_start(
            out=w2b[:], in_=w2r[ot]).then_inc(s_w2.h, 16))
        s_w2.bump(16)
        arow, brow = lt_pair(k, "w2")
        P.wait("tensor", s_lt, lt_ready)
        first_stt = True
        for strip in range(8):
            bank = strip % 2
            off = bank * 512
            P.wait("tensor", s_v, st["psD_cons"][bank])
            for j in range(4):
                ht = strip * 4 + j
                cblk = ht // 8
                st0 = 512 * (ot % 2) + cblk
                ins = lambda e, j=j, off=off, arow=arow, brow=brow, st0=st0, ht=ht: e.matmul(
                    psD[:, off + j * 128: off + (j + 1) * 128],
                    arow[:, (ht % 8) * 128:(ht % 8) * 128 + 128],
                    brow[:, st0:st0 + 509:4], start=True, stop=True)
                if j == 3:
                    P.add("tensor", lambda e, f=ins: f(e).then_inc(s_t.h, 1))
                else:
                    P.add("tensor", ins)
            s_t.bump()
            P.wait("vector", s_t, s_t.n)
            P.wait("vector", s_w2, s_w2.n)
            if first_stt:
                P.wait("vector", s_t, st["w2e_cons"][slot])
                P.wait("vector", s_sw, st["w2_sw"].get(ot - 2, 0))
                first_stt = False
            P.add("vector", lambda e, strip=strip, slot=slot, off=off: e.scalar_tensor_tensor(
                out=w2e[:, slot * 4096 + strip * 512: slot * 4096 + (strip + 1) * 512],
                in0=psD[:, off:off + 512], scalar=sclc[:, 0:1],
                in1=w2b[:, strip * 512:(strip + 1) * 512], op0=ALU.mult,
                op1=ALU.add).then_inc(s_v.h, 1))
            s_v.bump()
            st["psD_cons"][bank] = s_v.n
        st["w2_stt"] = s_v.n
        P.wait("gpsimd", s_v, s_v.n)
        P.add("gpsimd", lambda e, ot=ot, slot=slot: e.dma_start(
            out=w2scr[ot], in_=w2e[:, slot * 4096:(slot + 1) * 4096]).then_inc(s_sw.h, 16))
        s_sw.bump(16)
        st["w2_sw"][ot] = s_sw.n
        return s_v.n

    def w2_load(ot):
        slot = ot % 2
        P.wait("sync", s_t, st["w2e_cons"][slot])
        P.wait("sync", s_sw, st["w2_sw"][ot])
        P.add("sync", lambda e, ot=ot, slot=slot: e.dma_start(
            out=w2e[:, slot * 4096:(slot + 1) * 4096], in_=w2scr[ot]).then_inc(s_sc.h, 16))
        s_sc.bump(16)
        return s_sc.n

    def ffn2(c, lt_ready):
        h_ready = st["h_ready"]
        ready = {}
        for ot in range(NOT + 1):
            if ot < NOT:
                if c == 0:
                    ready[ot] = ("v", w2_build(ot, lt_ready))
                else:
                    ready[ot] = ("sc", w2_load(ot))
            if ot == 0:
                continue
            mo = ot - 1
            slot = mo % 2
            kind, rv = ready.pop(mo)
            pA, pB = psA[slot * 2], psA[slot * 2 + 1]
            P.wait("tensor", s_v if kind == "v" else s_sc, rv)
            P.wait("tensor", s_s, h_ready)
            P.wait("tensor", s_s, st["psA_cons_s"][slot * 2])
            P.wait("tensor", s_s, st["psA_cons_s"][slot * 2 + 1])
            for ht in range(NHT):
                for half, pp in ((0, pA), (1, pB)):
                    ins = lambda e, ht=ht, slot=slot, pp=pp, half=half: e.matmul(
                        pp[:], w2e[:, slot * 4096 + ht * 128: slot * 4096 + (ht + 1) * 128],
                        htile(ht)[:, half * 512:(half + 1) * 512],
                        start=(ht == 0), stop=(ht == NHT - 1))
                    if ht == NHT - 1 and half == 1:
                        P.add("tensor", lambda e, f=ins: f(e).then_inc(s_t.h, 1))
                    else:
                        P.add("tensor", ins)
            s_t.bump()
            mm_done = s_t.n
            st["w2e_cons"][slot] = mm_done
            st["h_cons_t"] = mm_done
            P.wait("scalar", s_t, mm_done)
            P.wait("scalar", s_v, st["tmp_free_v"])
            P.wait("scalar", s_v, prologue_v)
            for half, pp in ((0, pA), (1, pB)):
                ins = lambda e, mo=mo, pp=pp, half=half: e.activation(
                    tmp[:, half * 512:(half + 1) * 512], pp[:], AF.Identity,
                    bias=hb2[:, mo:mo + 1], scale=0.5)
                if half == 1:
                    P.add("scalar", lambda e, f=ins: f(e).then_inc(s_s.h, 1))
                else:
                    P.add("scalar", ins)
            s_s.bump()
            st["psA_cons_s"][slot * 2] = s_s.n
            st["psA_cons_s"][slot * 2 + 1] = s_s.n
            st["ep_last_s"] = s_s.n
            P.wait("vector", s_s, s_s.n)
            P.add("vector", lambda e, mo=mo, c=c: e.tensor_tensor(
                out=xr(mo)[:, c * CH:(c + 1) * CH], in0=xr(mo)[:, c * CH:(c + 1) * CH],
                in1=tmp[:, 0:1024], op=ALU.add).then_inc(s_v.h, 1))
            s_v.bump()
            st["tmp_free_v"] = s_v.n
        st["x_ready"] = s_v.n

    def ffn_call(call):
        st["xn_ready"] = [0, 0]
        st["w1_sw"] = {}
        st["w2_sw"] = {}
        st["w1_stt2"] = {}
        for s in range(2):
            bc = stats_sub(0, s, deferred=False)
            st["xn_ready"][0] = normalize_sub(0, s, bc, deferred=False)
        lt_phase(call)
        bc_c1s0 = stats_sub(1, 0, deferred=True)
        lt_ready = lt_readback(call)
        ffn1(0, lt_ready)
        ffn2(0, lt_ready)
        n0 = normalize_sub(1, 0, bc_c1s0, deferred=True)
        P.wait("tensor", s_v, n0)
        bc_c1s1 = stats_sub(1, 1, deferred=True)
        st["xn_ready"][1] = normalize_sub(1, 1, bc_c1s1, deferred=True)
        ffn1(1, lt_ready)
        ffn2(1, lt_ready)

    def attn_phase():
        P.wait("vector", s_s, prologue_s)

        first = True
        norm_ms = {}
        for s4 in range(4):
            c, ss = divmod(s4, 2)
            bc = stats_sub(c, ss, deferred=False, xf32=True)
            rs_bc = psD[:, 0:512]
            q = s4 * SUB
            P.wait("vector", s_t, bc)
            if first:
                P.wait("vector", s_t, st["h_cons_t"])
                P.wait("tensor", s_s, st["psA_cons_s"][0])
                P.wait("tensor", s_s, st["psA_cons_s"][1])
                first = False
            mrs_bc = psD[:, 512:1024]
            for dt in range(NDT):
                nA = ntmp[:, 0:512]
                nB = ntmp[:, 512:1024]
                P.add("vector", lambda e, dt=dt, q=q, rs=rs_bc, nA=nA: e.tensor_tensor(
                    out=nA, in0=xr(dt)[:, q:q + SUB], in1=rs, op=ALU.mult))
                P.add("vector", lambda e, mrs=mrs_bc, nA=nA: e.tensor_tensor(
                    out=nA, in0=nA, in1=mrs, op=ALU.subtract))
                P.add("vector", lambda e, dt=dt, nA=nA, nB=nB: e.tensor_scalar(
                    out=nB, in0=nA, scalar1=bv[:, dt:dt + 1], scalar2=None, op0=ALU.mult))
                if s4 > 0:
                    P.add("vector", lambda e, dt=dt, nB=nB: e.scalar_tensor_tensor(
                        out=nB[:, 0:1], in0=ucarS[:, dt:dt + 1],
                        scalar=ahcol[:, dt:dt + 1], in1=nB[:, 0:1],
                        op0=ALU.mult, op1=ALU.add))
                P.add("vector", lambda e, dt=dt: e.tensor_scalar(
                    out=ahbcT[:], in0=xr(0)[:, 0:512], scalar1=0.0,
                    scalar2=ahcol[:, dt:dt + 1], op0=ALU.mult, op1=ALU.add))
                ins2 = lambda e, dt=dt, q=q, nB=nB: e.tensor_tensor_scan(
                    out=ytile(dt)[:, q:q + SUB], data0=ahbcT[:],
                    data1=nB, initial=0.0, op0=ALU.mult, op1=ALU.add)
                P.add("vector", lambda e, f=ins2: f(e).then_inc(s_v.h, 1))
                s_v.bump()
                norm_ms[dt] = s_v.n
            st["psD_cons"][0] = s_v.n
            st["psD_cons"][1] = s_v.n
            for dt in range(NDT):
                ins3 = lambda e, dt=dt, q=q: e.tensor_copy(
                    ucarS[:, dt:dt + 1], ytile(dt)[:, q + SUB - 1:q + SUB])
                if dt == NDT - 1:
                    P.add("vector", lambda e, f=ins3: f(e).then_inc(s_v.h, 1))
                else:
                    P.add("vector", ins3)
            s_v.bump()
            P.wait("vector", s_v, s_v.n)
        for dt in range(NDT):
            ins = lambda e, dt=dt: e.tensor_scalar(
                out=u_sb[:, dt:dt + 1], in0=ucarS[:, dt:dt + 1],
                scalar1=mevc[:, 0:1], scalar2=None, op0=ALU.mult)
            if dt == NDT - 1:
                P.add("vector", lambda e, f=ins: f(e).then_inc(s_v.h, 1))
            else:
                P.add("vector", ins)
        s_v.bump()
        P.wait("gpsimd", s_v, s_v.n)
        P.add("gpsimd", lambda e: e.dma_start(out=cc_u_in.ap(), in_=u_sb[:]).then_inc(s_u.h, 16))
        s_u.bump(16)
        P.wait("gpsimd", s_u, s_u.n)
        P.add("gpsimd", lambda e: e.collective_compute(
            "AllReduce", ALU.add, replica_groups=[[0, 1], [2, 3], [4, 5], [6, 7]],
            ins=[cc_u_in.ap().opt()], outs=[cc_u_out.ap().opt()]).then_inc(s_cc.h, 1))
        s_cc.bump()
        zeros = x2s[:].bitcast(F32)[:, 0:512]
        P.wait("vector", s_t, st["mid_free_t"])
        P.add("vector", lambda e: e.memset(zeros, 0.0))
        for dt in range(NDT):
            P.add("vector", lambda e, dt=dt: e.tensor_scalar(
                out=ahbcT[:], in0=xr(0)[:, 0:512], scalar1=0.0,
                scalar2=ahcol[:, dt:dt + 1], op0=ALU.mult, op1=ALU.add))
            ins = lambda e, dt=dt: e.tensor_tensor_scan(
                out=mid[:, dt * 512:(dt + 1) * 512], data0=ahbcT[:], data1=zeros,
                initial=1.0, op0=ALU.mult, op1=ALU.add)
            if dt == NDT - 1:
                P.add("vector", lambda e, f=ins: f(e).then_inc(s_v.h, 1))
            else:
                P.add("vector", ins)
        s_v.bump()
        pow_ready = s_v.n
        P.wait("gpsimd", s_t, st["w2e_cons"][0])
        P.wait("gpsimd", s_t, st["w2e_cons"][1])
        for et in range(NET):
            P.add("gpsimd", lambda e, et=et: e.dma_start(
                out=w2e[:, et * 1024:(et + 1) * 1024], in_=wor[et]).then_inc(s_wo.h, 16))
            s_wo.bump(16)
        wo_ld = s_wo.n
        P.wait("gpsimd", s_cc, s_cc.n)
        P.add("gpsimd", lambda e: e.dma_start(out=u_res[:], in_=cc_u_out.ap()).then_inc(s_u.h, 16))
        s_u.bump(16)
        P.wait("vector", s_u, s_u.n)
        P.add("vector", lambda e: e.tensor_scalar(
            out=u_res[:], in0=u_res[:], scalar1=modc[:, 0:1], scalar2=None,
            op0=ALU.mult).then_inc(s_v.h, 1))
        s_v.bump()
        P.wait("vector", s_v, s_v.n)
        ybf_ms = []
        for pch in range(4):
            q = pch * SUB
            for dt in range(NDT):
                initc = u_res[:, dt:dt + 1] if pch == 0 else ucar[:, dt:dt + 1]
                ins = lambda e, dt=dt, q=q, pch=pch, initc=initc: e.scalar_tensor_tensor(
                    out=ybf(dt, pch), in0=mid[:, dt * 512:(dt + 1) * 512],
                    scalar=initc, in1=ytile(dt)[:, q:q + SUB],
                    op0=ALU.mult, op1=ALU.add)
                if dt == NDT - 1:
                    P.add("vector", lambda e, f=ins: f(e).then_inc(s_v.h, 1))
                else:
                    P.add("vector", ins)
                P.add("vector", lambda e, dt=dt, initc=initc: e.tensor_scalar(
                    out=ucar[:, dt:dt + 1], in0=initc, scalar1=ahp512[:, dt:dt + 1],
                    scalar2=None, op0=ALU.mult))
            s_v.bump()
            ybf_ms.append(s_v.n)
            P.wait("vector", s_v, s_v.n)
        proj_started = False
        last_mm = 0
        for pch in range(4):
            for et in range(NET):
                bank = et % 4
                pp = psA[bank]
                P.wait("tensor", s_wo, wo_ld)
                P.wait("tensor", s_v, ybf_ms[pch])
                if not proj_started:
                    P.wait("tensor", s_s, st["ep_last_s"])
                    P.wait("tensor", s_v, pow_ready)
                    proj_started = True
                P.wait("tensor", s_s, st["psA_cons_s"][bank])
                for dt in range(NDT):
                    ins = lambda e, dt=dt, et=et, pp=pp, pch=pch: e.matmul(
                        pp[:], w2e[:, et * 1024 + dt * 128: et * 1024 + (dt + 1) * 128],
                        ybf(dt, pch), start=(dt == 0), stop=(dt == NDT - 1))
                    if dt == NDT - 1:
                        P.add("tensor", lambda e, f=ins: f(e).then_inc(s_t.h, 1))
                    else:
                        P.add("tensor", ins)
                s_t.bump()
                mm_done = s_t.n
                last_mm = mm_done
                P.wait("scalar", s_t, mm_done)
                P.wait("scalar", s_v, st["tmp_free_v"])
                P.add("scalar", lambda e, et=et, pp=pp: e.activation(
                    tmp[:, 0:512], pp[:], AF.Identity,
                    bias=bov[:, et:et + 1]).then_inc(s_s.h, 1))
                s_s.bump()
                st["psA_cons_s"][bank] = s_s.n
                P.wait("vector", s_s, s_s.n)
                P.add("vector", lambda e, et=et, pch=pch: e.tensor_tensor(
                    out=xr(et)[:, pch * SUB:(pch + 1) * SUB],
                    in0=xr(et)[:, pch * SUB:(pch + 1) * SUB],
                    in1=tmp[:, 0:512], op=ALU.add).then_inc(s_v.h, 1))
                s_v.bump()
                st["tmp_free_v"] = s_v.n
        st["big_free_t"] = last_mm
        st["w2e_cons"] = [last_mm, last_mm]
        st["mid_free_t"] = 0
        st["x_ready"] = s_v.n

    import os
    stop = os.environ.get("KSTOP", "")
    ffn_call(0)
    if stop != "ffn0":
        attn_phase()
        if stop != "attn":
            ffn_call(1)

    P.wait("sync", s_v, st["x_ready"])
    for dt in range(NDT):
        P.add("sync", lambda e, dt=dt: e.dma_start(
            out=out_cm[dt * 128:(dt + 1) * 128, :], in_=xr(dt)).then_inc(s_st.h, 16))
        s_st.bump(16)
    P.wait("sync", s_st, s_st.n)

    with ctx:
        with nc.Block() as block:
            @block.sync
            def _(e):
                for fn in P.q["sync"]:
                    fn(e)

            @block.gpsimd
            def _(e):
                for fn in P.q["gpsimd"]:
                    fn(e)

            @block.vector
            def _(e):
                for fn in P.q["vector"]:
                    fn(e)

            @block.scalar
            def _(e):
                for fn in P.q["scalar"]:
                    fn(e)

            @block.tensor
            def _(e):
                for fn in P.q["tensor"]:
                    fn(e)

    return nc


def kernel(L0_attn_x, L0_attn_a, L0_attn_b, L0_attn_w_out, L0_attn_b_out,
           L0_ffn_W1, L0_ffn_b1, L0_ffn_W2, L0_ffn_b2, L0_ffn_scale):
    x = np.asarray(L0_attn_x, dtype=np.float32)
    W1 = np.asarray(L0_ffn_W1, np.float32)
    W2 = np.asarray(L0_ffn_W2, np.float32)
    wo = np.asarray(L0_attn_w_out, np.float32)
    a = np.asarray(L0_attn_a, np.float32)
    b = np.asarray(L0_attn_b, np.float32)
    b_out = np.asarray(L0_attn_b_out, np.float32)
    b1 = np.asarray(L0_ffn_b1, np.float32)
    b2 = np.asarray(L0_ffn_b2, np.float32)
    scale = np.float32(np.asarray(L0_ffn_scale).reshape(()))

    w1r = np.ascontiguousarray(np.transpose(W1.reshape(32, 128, 8, 128), (0, 3, 2, 1)))
    w2r = np.ascontiguousarray(np.transpose(W2.reshape(8, 128, 32, 128), (0, 3, 2, 1)))
    wor = np.ascontiguousarray(np.transpose(wo.reshape(8, 128, 8, 128), (0, 3, 2, 1)))
    a_pm = np.ascontiguousarray(a.reshape(8, 128).T)
    b_pm = np.ascontiguousarray(b.reshape(8, 128).T)
    bo_pm = np.ascontiguousarray(b_out.reshape(8, 128).T)
    b1_pm = np.ascontiguousarray(b1.reshape(32, 128).T)
    b2_pm = np.ascontiguousarray(b2.reshape(8, 128).T)
    scale_col = np.full((128, 1), scale, np.float32)
    ident = np.eye(128, dtype=ml_dtypes.bfloat16)

    nc = build_program()
    in_maps = []
    for core in range(8):
        bidx, half = divmod(core, 2)
        xs = np.ascontiguousarray(x[bidx, half * T:(half + 1) * T, :].T)
        in_maps.append({
            "x_cm": xs, "w1r": w1r, "w2r": w2r, "wor": wor,
            "a_pm": a_pm, "b_pm": b_pm, "bo_pm": bo_pm, "b1_pm": b1_pm,
            "b2_pm": b2_pm, "scale_col": scale_col,
            "mask_even": np.full((128, 1), 1.0 if half == 0 else 0.0, np.float32),
            "mask_odd": np.full((128, 1), 1.0 if half == 1 else 0.0, np.float32),
            "ident_d": ident,
        })
    res = run_bass_kernel_spmd(nc, in_maps, core_ids=list(range(8)))
    out = np.empty((4, 4096, 1024), np.float32)
    for core in range(8):
        bidx, half = divmod(core, 2)
        out[bidx, half * T:(half + 1) * T, :] = res.results[core]["out_cm"].T
    return out
